# version 01
# baseline (speedup 1.0000x reference)
import numpy as np
import ml_dtypes
import concourse.bass as bass
import concourse.mybir as mybir
from concourse.bass_utils import run_bass_kernel_spmd

F32 = mybir.dt.float32
BF16 = mybir.dt.bfloat16
I32 = mybir.dt.int32
AF = mybir.ActivationFunctionType
OP = mybir.AluOpType
AX = mybir.AxisListType.X
AP = bass.AP

NCORES = 8
B, T, D = 16, 2048, 2048
ST = 512
NT = 4
NST = T // ST
DC = D // 128
I8K = 8192

ATTN_DONE_ONLY = False
DEBUG_TAPS = False
DBG_ST = 0

EPS_LN = 1e-5
CL1 = float(1e-5 * np.sqrt(2048.0))
CL2 = 1e-5
RT2048 = float(1.0 / np.sqrt(2048.0))
RT8192 = float(1.0 / np.sqrt(8192.0))


def _hadamard(n):
    h = np.array([[1.0]], dtype=np.float32)
    while h.shape[0] < n:
        h = np.block([[h, h], [h, -h]])
    return h


def _consts():
    bf = ml_dtypes.bfloat16
    j = np.arange(128)
    return dict(
        cH128=np.asarray(_hadamard(128), dtype=bf),
        cM16=np.asarray(np.kron(np.eye(8, dtype=np.float32), _hadamard(16)), dtype=bf),
        cM64=np.asarray(np.kron(np.eye(2, dtype=np.float32), _hadamard(64)), dtype=bf),
        cMaskC=np.asarray((j[:, None] <= j[None, :]), dtype=bf),
        cMaskP=np.asarray((j[:, None] >= j[None, :]), dtype=bf),
        cIdent=np.eye(128, dtype=np.float32),
    )


class KB:

    def __init__(self):
        self.nc = bass.Bass("TRN2", target_bir_lowering=False)
        self.sems = {}
        self.cnt = {}
        self.waited = {}
        self.t = {}
        self.ev_store = {}

    def inc(self, sem, instr, n=1):
        instr.then_inc(self.sems[sem], n)
        self.cnt[sem] += n
        return (sem, self.cnt[sem])

    def ev(self, sem):
        return (sem, self.cnt[sem])

    def allev(self):
        return tuple((s, self.cnt[s]) for s in self.sems)

    def wait(self, eng, engname, *evs):
        wd = self.waited.setdefault(engname, {})
        for e in evs:
            if e is None:
                continue
            if isinstance(e, tuple) and isinstance(e[0], tuple):
                self.wait(eng, engname, *e)
                continue
            sem, val = e
            if val > 0 and wd.get(sem, 0) < val:
                eng.wait_ge(self.sems[sem], val)
                wd[sem] = val

    def dma(self, eng, engname, sem, dst, src, n=16):
        self.wait(eng, engname, self.ev("D2"))
        return self.inc(sem, eng.dma_start(dst, src), n)

    def dmat(self, eng, dst, src):
        self.wait(eng, "scalar", self.ev("D1"), self.ev("G"))
        return self.inc("D2", eng.dma_start(dst, src, transpose=True), 16)


def build_program():
    kb = KB()
    nc = kb.nc
    dt = nc.dram_tensor
    t = kb.t

    t["xin"] = dt("xin", [2, T, D], F32, kind="ExternalInput").ap()
    t["wattn"] = dt("wattn", [3 * D, D], F32, kind="ExternalInput").ap()
    t["wo"] = dt("wo", [D, D], F32, kind="ExternalInput").ap()
    t["wg"] = dt("wg", [I8K, D], F32, kind="ExternalInput").ap()
    t["wu"] = dt("wu", [I8K, D], F32, kind="ExternalInput").ap()
    t["wd"] = dt("wd", [D, I8K], F32, kind="ExternalInput").ap()
    t["cv"] = dt("cv", [2, D], F32, kind="ExternalInput").ap()
    t["wmod"] = dt("wmod", [D, 6 * D], F32, kind="ExternalInput").ap()
    for n in ("cH128", "cM16", "cM64", "cMaskC", "cMaskP"):
        t[n] = dt(n, [128, 128], BF16, kind="ExternalInput").ap()
    t["cIdent"] = dt("cIdent", [128, 128], F32, kind="ExternalInput").ap()
    t["xout"] = dt("xout", [2, T, D], F32, kind="ExternalOutput").ap()
    if DEBUG_TAPS:
        for n in ("thb", "tqb", "tkb", "tvb", "tyb", "tat", "tkc", "tvc"):
            t[n] = dt(n, [128, I8K], BF16, kind="ExternalOutput").ap()
        t["tmid"] = dt("tmid", [ST, I8K], BF16, kind="ExternalOutput").ap()
        t["ty2"] = dt("ty2", [128, I8K], BF16, kind="ExternalOutput").ap()
        t["tlv2"] = dt("tlv2", [128, I8K], BF16, kind="ExternalOutput").ap()

    t["WATT"] = dt("WATT", [D * 3 * D], BF16, kind="Internal").ap()
    t["WOs"] = dt("WOs", [D * D], BF16, kind="Internal").ap()
    t["WGs"] = dt("WGs", [D * I8K], BF16, kind="Internal").ap()
    t["WUs"] = dt("WUs", [D * I8K], BF16, kind="Internal").ap()
    t["WDs"] = dt("WDs", [I8K * D], BF16, kind="Internal").ap()
    t["ZB"] = dt("ZB", [128 * I8K], BF16, kind="Internal").ap()
    t["MIDH"] = dt("MIDH", [ST * I8K], BF16, kind="Internal").ap()
    t["LV2H"] = dt("LV2H", [I8K * ST], BF16, kind="Internal").ap()
    t["MODS"] = dt("MODS", [2 * 6 * D], F32, kind="Internal").ap()
    t["SCAL"] = dt("SCAL", [64], F32, kind="Internal").ap()

    ctx = []

    def sb(name, p, f, dtype=F32):
        h = nc.sbuf_tensor(name, [p, f], dtype)
        ctx.append(h)
        t[name] = h.__enter__()
        return t[name]

    def psm(name, p, f):
        h = nc.psum_tensor(name, [p, f], F32)
        ctx.append(h)
        t[name] = h.__enter__()
        return t[name]

    XP = sb("XP", 128, NT * D, F32)
    t["xb"] = t["pw"] = XP
    sb("wst", 128, 2 * DC * 512, BF16)
    U1 = sb("U1", 128, I8K, BF16)
    t["qb"] = t["z1"] = t["lvb"] = t["mst"] = t["sqd"] = U1
    U2 = sb("U2", 128, I8K, BF16)
    t["ptt"] = t["kb"] = t["zl2"] = U2
    U3 = sb("U3", 128, I8K, BF16)
    t["ptd"] = t["vb"] = t["mf"] = U3
    U4 = sb("U4", 128, I8K, BF16)
    t["hF"] = t["yb"] = t["y1"] = t["lvF"] = t["lv2"] = U4
    U5 = sb("U5", 128, I8K, BF16)
    t["hb"] = t["y2"] = t["lv2F"] = U5
    S1 = sb("S1", 128, D, F32)
    t["dmp"] = t["rfb"] = t["rb"] = S1
    S2 = sb("S2", 128, D, F32)
    t["dm2"] = t["ddb"] = S2
    S3 = sb("S3", 128, D, F32)
    t["bc1"] = t["ttb"] = S3
    S4 = sb("S4", 128, D, F32)
    t["bc2"] = t["silu"] = S4
    sb("bc3", 128, D, F32)
    sb("rib", 128, D, I32)
    sb("kc", 128, DC * 128, BF16)
    sb("vc", 128, D, BF16)
    sb("atm", 128, 5 * 256, BF16)
    sb("silc", 128, DC * 2, BF16)
    sb("silc32", 128, DC * 2, F32)
    sb("rows", 128, 32, F32)
    for n in ("r_m", "r_v", "r_s", "r_t0", "r_t1", "r_t2", "r_t3"):
        sb(n, 128, 1, F32)
    sb("r_sc", 128, NT, F32)
    sb("r_dg", 128, NT, F32)
    sb("r_cu", 128, NT, F32)
    sb("r_d2", 128, 1, F32)
    sb("wsc", 128, 8, F32)
    sb("H128s", 128, 128, BF16)
    sb("M16s", 128, 128, BF16)
    sb("M64s", 128, 128, BF16)
    sb("mCs", 128, 128, BF16)
    sb("mPs", 128, 128, BF16)
    sb("idns", 128, 128, F32)
    sb("pstg", 128, 128, F32)
    sb("pscl", 1, 8, F32)

    for i in range(8):
        psm(f"pb{i}", 128, 512)

    with nc.Block() as block:
        kb.block = block
        for s in ("V", "A", "T", "G", "D1", "D2"):
            h = nc.semaphore("sem" + s)
            ctx.append(h)
            kb.sems[s] = h.__enter__()
            kb.cnt[s] = 0

        emit(kb)

    for h in reversed(ctx):
        h.__exit__(None, None, None)
    return nc


def emit(kb):
    t = kb.t
    blk = kb.block

    @blk.sync
    def _(e):
        for n in ("cH128", "cM16", "cM64", "cMaskC", "cMaskP", "cIdent"):
            dst = {"cH128": "H128s", "cM16": "M16s", "cM64": "M64s",
                   "cMaskC": "mCs", "cMaskP": "mPs", "cIdent": "idns"}[n]
            kb.dma(e, "sync", "D1", t[dst][:], t[n][:])

    emit_mod(kb)
    emit_wprep(kb)
    if not ATTN_DONE_ONLY:
        emit_ffn_prep(kb)

    for st in range(2 * NST):
        emit_supertile(kb, st)

    fin = kb.allev()

    @blk.sync
    def _(e):
        kb.wait(e, "sync", *fin)


def emit_mod(kb):
    t = kb.t
    blk = kb.block
    nc = kb.nc
    pb = [t[f"pb{i}"] for i in range(8)]

    @blk.gpsimd
    def _(e):
        with nc.allow_non_contiguous_dma("tiny cT load"):
            for fc in range(DC):
                src = AP(t["cv"].tensor, fc * 128, [[1, 128], [D, 2]])
                dst = AP(t["silc"], fc * 2, [[DC * 2, 128], [1, 2]])
                kb.dma(e, "gpsimd", "G", dst, src)
    ev_c = kb.ev("G")

    @blk.scalar
    def _(e):
        kb.wait(e, "scalar", ev_c)
        n = kb.inc("A", e.activation(t["silc32"][:], t["silc"][:], AF.Silu))
        kb.wait(e, "scalar", n)
        kb.inc("A", e.copy(t["silc"][:], t["silc32"][:]))
    ev_sil = kb.ev("A")

    nblk = 6 * D // 512
    ev_mm_prev = {}
    ev_grp_store = {}
    for j in range(nblk):
        par = j % 2
        g = j // 4

        gates = g in (2, 5)

        @blk.gpsimd
        def _(e, j=j, par=par, gates=gates):
            if j >= 2:
                kb.wait(e, "gpsimd", ev_mm_prev.get(j - 2))
            if gates:
                kb.wait(e, "gpsimd", ev_mm_prev.get(j - 1))
                src = AP(t["wmod"].tensor, j * 512, [[6 * D, 128], [128 * 6 * D, DC], [1, 512]])
                dst = AP(t["pw"], 0, [[I8K, 128], [512, DC], [1, 512]])
            else:
                src = AP(t["wmod"].tensor, j * 512, [[6 * D, 128], [128 * 6 * D, DC], [1, 512]])
                dst = AP(t["wst"], par * DC * 512, [[2 * DC * 512, 128], [512, DC], [1, 512]])
            kb.dma(e, "gpsimd", "G", dst, src)
        ev_w = kb.ev("G")

        @blk.tensor
        def _(e, j=j, par=par, gates=gates, ev_w=ev_w):
            kb.wait(e, "tensor", ev_w, ev_sil)
            for fc in range(DC):
                if gates:
                    lhs = t["silc32"][:, fc * 2:(fc + 1) * 2]
                    rhs = AP(t["pw"], fc * 512, [[I8K, 128], [1, 512]])
                else:
                    lhs = t["silc"][:, fc * 2:(fc + 1) * 2]
                    rhs = AP(t["wst"], par * DC * 512 + fc * 512,
                             [[2 * DC * 512, 128], [1, 512]])
                kb.inc("T", e.matmul(pb[par][0:2, 0:512], lhs, rhs,
                                     start=(fc == 0), stop=(fc == DC - 1)))
        ev_mm = kb.ev("T")

        @blk.vector
        def _(e, j=j, par=par, g=g, ev_mm=ev_mm):
            kb.wait(e, "vector", ev_mm)
            kb.wait(e, "vector", ev_grp_store.get(g - 1))
            ev_mm_prev[j] = kb.inc("V", e.tensor_copy(
                t["dmp"][0:2, (j % 4) * 512:(j % 4 + 1) * 512], pb[par][0:2, 0:512]))

        if j % 4 == 3:
            ev_cp = kb.ev("V")

            @blk.vector
            def _(e, g=g, ev_cp=ev_cp):
                kb.wait(e, "vector", ev_cp)
                s = t["dmp"][0:2, 0:D]
                if g in (2, 5):
                    sgn = t["dm2"][0:2, 0:D]
                    u = t["bc3"][0:2, 0:D]
                    n = kb.inc("V", e.tensor_scalar(sgn, s, 0.0, 2.0, OP.is_ge, OP.mult))
                    kb.wait(e, "vector", n)
                    n = kb.inc("V", e.scalar_tensor_tensor(u, s, -1.0, s, OP.mult, OP.max))
                    kb.wait(e, "vector", n)
                    n = kb.inc("V", e.tensor_scalar_max(u, u, 0.25))
                    kb.wait(e, "vector", n)
                    n = kb.inc("V", e.tensor_tensor(s, sgn, u, OP.mult))
                    kb.wait(e, "vector", n)
                    n = kb.inc("V", e.tensor_tensor(s, s, u, OP.subtract))
                    kb.wait(e, "vector", n)
                elif g in (1, 4):
                    n = kb.inc("V", e.tensor_scalar_add(s, s, 1.0))
                    kb.wait(e, "vector", n)
            ev_tr = kb.ev("V")

            @blk.gpsimd
            def _(e, g=g, ev_tr=ev_tr):
                kb.wait(e, "gpsimd", ev_tr)
                dst = AP(t["MODS"].tensor, g * D, [[6 * D, 2], [1, D]])
                ev_grp_store[g] = kb.dma(e, "gpsimd", "G", dst, t["dmp"][0:2, 0:D])
    kb.ev_store["mods"] = kb.ev("G")
    kb.ev_store["wstf"] = {0: ev_mm_prev[18], 1: ev_mm_prev[19]}


def emit_bcast(kb, b, which, waits=()):
    t = kb.t
    blk = kb.block
    slots = {"ln1": [(1, "bc1"), (0, "bc2")], "ln2": [(4, "bc1"), (3, "bc2")],
             "ga": [(2, "bc3")], "gb": [(5, "bc3")]}[which]
    out = [None]

    @blk.gpsimd
    def _(e):
        kb.wait(e, "gpsimd", kb.ev_store["mods"], *waits)
        for j, slot in slots:
            src = AP(t["MODS"].tensor, b * 6 * D + j * D, [[0, 128], [1, D]])
            out[0] = kb.dma(e, "gpsimd", "G", t[slot][:], src)
    return out[0]


def emit_wprep(kb):
    t = kb.t
    blk = kb.block
    done = {}
    mod_done = (("T", kb.cnt["T"]), ("V", kb.cnt["V"]))
    for wi, (win, wout, nrow, ncol) in enumerate(
            ((t["wattn"], t["WATT"], 3 * D, D), (t["wo"], t["WOs"], D, D))):
        for ch in range(nrow // 128):
            par = ch % 2

            @blk.gpsimd
            def _(e, wi=wi, ch=ch, par=par, win=win, ncol=ncol):
                kb.wait(e, "gpsimd", *mod_done)
                kb.wait(e, "gpsimd", kb.ev_store.get("wstf", {}).get(par))
                if ch >= 2:
                    kb.wait(e, "gpsimd", done.get((wi, ch - 2)))
                src = AP(win.tensor, ch * 128 * ncol, [[ncol, 128], [1, ncol]])
                dst = AP(t["wst"], par * DC * 512, [[2 * DC * 512, 128], [1, ncol]])
                kb.dma(e, "gpsimd", "G", dst, src)
            ev_ld = kb.ev("G")

            @blk.scalar
            def _(e, par=par, ncol=ncol, ev_ld=ev_ld):
                kb.wait(e, "scalar", ev_ld)
                src = AP(t["wst"], par * DC * 512, [[2 * DC * 512, 128], [1, ncol]])
                dst = AP(t["ptd"], 0, [[I8K, 128], [128, ncol // 128], [1, 128]])
                kb.dmat(e, dst, src)
            ev_dt = kb.ev("D2")

            @blk.gpsimd
            def _(e, wi=wi, ch=ch, wout=wout, ncol=ncol, nrow=nrow, ev_dt=ev_dt):
                kb.wait(e, "gpsimd", ev_dt)
                src = AP(t["ptd"], 0, [[I8K, 128], [128, ncol // 128], [1, 128]])
                dst = AP(wout.tensor, ch * 128, [[nrow, 128], [128 * nrow, ncol // 128], [1, 128]])
                done[(wi, ch)] = kb.dma(e, "gpsimd", "G", dst, src)
    kb.ev_store["wprep"] = kb.ev("G")
    kb.ev_store["wstf"] = {0: kb.ev("G"), 1: kb.ev("G")}


def emit_ffn_prep(kb):
    t = kb.t
    blk = kb.block
    pb = [t[f"pb{i}"] for i in range(8)]
    pw_free = (("T", kb.cnt["T"]), ("V", kb.cnt["V"]))
    specs = [(t["wg"], t["WGs"], I8K, D, 0), (t["wu"], t["WUs"], I8K, D, 1),
             (t["wd"], t["WDs"], D, I8K, 2)]
    for si, (win, wout, nrow, ncol, scol) in enumerate(specs):
        nch = nrow // 128

        @blk.vector
        def _(e):
            n = kb.inc("V", e.memset(t["pstg"][:], 0.0))
            kb.wait(e, "vector", n)

        red_done = {}
        for ch in range(nch):
            big = ncol > D
            slot = 0 if big else (ch % 2) * D

            @blk.sync
            def _(e, ch=ch, win=win, ncol=ncol, slot=slot, big=big):
                kb.wait(e, "sync", *pw_free)
                back = 1 if big else 2
                if ch >= back:
                    kb.wait(e, "sync", red_done.get(ch - back))
                src = AP(win.tensor, ch * 128 * ncol, [[ncol, 128], [1, ncol]])
                kb.dma(e, "sync", "D1", t["pw"][:, slot:slot + ncol], src)
            ev_ld = kb.ev("D1")

            @blk.vector
            def _(e, ch=ch, ncol=ncol, slot=slot, ev_ld=ev_ld):
                kb.wait(e, "vector", ev_ld)
                red_done[ch] = kb.inc("V", e.tensor_reduce(
                    t["pstg"][:, ch:ch + 1], t["pw"][:, slot:slot + ncol],
                    AX, OP.add, apply_absolute_value=True))
        ev_parts = kb.ev("V")

        @blk.vector
        def _(e, nch=nch):
            kb.wait(e, "vector", ev_parts)
            n = kb.inc("V", e.tensor_reduce(t["r_t0"][:], t["pstg"][:, 0:nch], AX, OP.add))
            kb.wait(e, "vector", n)
            n = kb.inc("V", e.memset(t["pstg"][:], 0.0))
            kb.wait(e, "vector", n)
            n = kb.inc("V", e.tensor_copy(t["pstg"][:, 0:1], t["r_t0"][:]))
            kb.wait(e, "vector", n)
        ev_col = kb.ev("V")

        @blk.tensor
        def _(e, ev_col=ev_col):
            kb.wait(e, "tensor", ev_col)
            kb.inc("T", e.transpose(pb[6][:, 0:128], t["pstg"][:, 0:128], t["idns"][:]))
        ev_tp = kb.ev("T")

        @blk.vector
        def _(e, nrow=nrow, ncol=ncol, ev_tp=ev_tp):
            kb.wait(e, "vector", ev_tp)
            n = kb.inc("V", e.tensor_reduce(t["pscl"][0:1, 0:1], pb[6][0:1, 0:128], AX, OP.add))
            kb.wait(e, "vector", n)
            n = kb.inc("V", e.tensor_scalar_mul(t["pscl"][0:1, 1:2], t["pscl"][0:1, 0:1],
                                                1.0 / (nrow * ncol)))
            kb.wait(e, "vector", n)
            n = kb.inc("V", e.tensor_scalar_max(t["pscl"][0:1, 2:3], t["pscl"][0:1, 1:2], 1e-5))
            kb.wait(e, "vector", n)
            n = kb.inc("V", e.tensor_scalar_mul(t["pscl"][0:1, 3:4], t["pscl"][0:1, 2:3], 0.5))
            kb.wait(e, "vector", n)
        ev_sc = kb.ev("V")

        @blk.gpsimd
        def _(e, scol=scol, ev_sc=ev_sc):
            kb.wait(e, "gpsimd", ev_sc)
            kb.dma(e, "gpsimd", "G", AP(t["SCAL"].tensor, 2 * scol, [[1, 1], [1, 2]]),
                   t["pscl"][0:1, 2:4])
        ev_st = kb.ev("G")

        @blk.gpsimd
        def _(e, scol=scol, ev_st=ev_st):
            kb.wait(e, "gpsimd", ev_st)
            kb.dma(e, "gpsimd", "G", t["wsc"][:, 2 * scol:2 * scol + 2],
                   AP(t["SCAL"].tensor, 2 * scol, [[0, 128], [1, 2]]))
        ev_wsc = kb.ev("G")

        @blk.vector
        def _(e, scol=scol, ev_wsc=ev_wsc):
            kb.wait(e, "vector", ev_wsc)
            n = kb.inc("V", e.tensor_scalar_mul(t["r_t3"][:], t["wsc"][:, 2 * scol + 1:2 * scol + 2], -1.0))
            kb.wait(e, "vector", n)
        ev_nt = kb.ev("V")

        st_done = {}
        for ch in range(nch):
            big = ncol > D
            slot = 0 if big else (ch % 2) * D

            @blk.sync
            def _(e, ch=ch, win=win, ncol=ncol, slot=slot, big=big, ev_nt=ev_nt):
                kb.wait(e, "sync", ev_nt)
                back = 1 if big else 2
                if ch >= back:
                    kb.wait(e, "sync", st_done.get(ch - back))
                src = AP(win.tensor, ch * 128 * ncol, [[ncol, 128], [1, ncol]])
                kb.dma(e, "sync", "D1", t["pw"][:, slot:slot + ncol], src)
            ev_ld = kb.ev("D1")

            @blk.vector
            def _(e, ch=ch, ncol=ncol, slot=slot, scol=scol, ev_ld=ev_ld):
                kb.wait(e, "vector", ev_ld)
                wv = t["pw"][:, slot:slot + ncol]
                p1t = t["y2"][:, 0:ncol]
                n = kb.inc("V", e.tensor_scalar(
                    p1t, wv, t["wsc"][:, 2 * scol + 1:2 * scol + 2], None, OP.is_ge))
                kb.wait(e, "vector", n)
                n = kb.inc("V", e.scalar_tensor_tensor(
                    t["ptt"][:, 0:ncol], wv, t["r_t3"][:], p1t, OP.is_le, OP.subtract))
                kb.wait(e, "vector", n)
            ev_tt = kb.ev("V")

            @blk.scalar
            def _(e, ncol=ncol, ev_tt=ev_tt):
                kb.wait(e, "scalar", ev_tt)
                src = AP(t["ptt"], 0, [[I8K, 128], [1, ncol]])
                dst = AP(t["ptd"], 0, [[I8K, 128], [128, ncol // 128], [1, 128]])
                kb.dmat(e, dst, src)
            ev_dt = kb.ev("D2")

            @blk.gpsimd
            def _(e, ch=ch, ncol=ncol, nrow=nrow, wout=wout, ev_dt=ev_dt):
                kb.wait(e, "gpsimd", ev_dt)
                src = AP(t["ptd"], 0, [[I8K, 128], [128, ncol // 128], [1, 128]])
                dst = AP(wout.tensor, ch * 128, [[nrow, 128], [128 * nrow, ncol // 128], [1, 128]])
                st_done[ch] = kb.dma(e, "gpsimd", "G", dst, src)
    kb.ev_store["ffnprep"] = kb.allev()


def emit_ln_mod(kb, ev_ready):
    t = kb.t
    blk = kb.block

    for tc in range(NT):
        xsl = t["xb"][:, tc * D:(tc + 1) * D]

        @blk.vector
        def _(e, tc=tc, xsl=xsl):
            kb.wait(e, "vector", *ev_ready)
            n = kb.inc("V", e.tensor_reduce(t["r_m"][:], xsl, AX, OP.add))
            kb.wait(e, "vector", n)
        ev_sum = kb.ev("V")

        @blk.scalar
        def _(e, tc=tc, xsl=xsl):
            kb.wait(e, "scalar", *ev_ready)
            n = kb.inc("A", e.activation(t["sqd"][:, 0:D], xsl, AF.Square,
                                         accum_out=t["r_v"][:]))
            kb.wait(e, "scalar", n)
        ev_sq = kb.ev("A")

        @blk.vector
        def _(e, tc=tc, ev_sq=ev_sq):
            kb.wait(e, "vector", ev_sq)
            n = kb.inc("V", e.tensor_scalar_mul(t["r_t0"][:], t["r_m"][:], 1.0 / D))
            kb.wait(e, "vector", n)
            n = kb.inc("V", e.tensor_scalar_mul(t["r_t1"][:], t["r_v"][:], 1.0 / D))
            kb.wait(e, "vector", n)
            n = kb.inc("V", e.tensor_tensor(t["r_t2"][:], t["r_t0"][:], t["r_t0"][:], OP.mult))
            kb.wait(e, "vector", n)
            n = kb.inc("V", e.tensor_tensor(t["r_v"][:], t["r_t1"][:], t["r_t2"][:], OP.subtract))
            kb.wait(e, "vector", n)
            n = kb.inc("V", e.tensor_scalar_add(t["r_v"][:], t["r_v"][:], EPS_LN))
            kb.wait(e, "vector", n)
        ev_var = kb.ev("V")

        @blk.scalar
        def _(e, ev_var=ev_var):
            kb.wait(e, "scalar", ev_var)
            n = kb.inc("A", e.activation(t["r_t2"][:], t["r_v"][:], AF.Sqrt))
            kb.wait(e, "scalar", n)
        ev_sig = kb.ev("A")

        @blk.vector
        def _(e, tc=tc, xsl=xsl, ev_sig=ev_sig):
            kb.wait(e, "vector", ev_sig)
            n = kb.inc("V", e.reciprocal(t["r_s"][:], t["r_t2"][:]))
            kb.wait(e, "vector", n)
            n = kb.inc("V", e.tensor_scalar(t["dmp"][:], xsl, t["r_t0"][:], t["r_s"][:],
                                            OP.subtract, OP.mult))
            kb.wait(e, "vector", n)
            n = kb.inc("V", e.tensor_tensor(t["dm2"][:], t["dmp"][:], t["bc1"][:], OP.mult))
            kb.wait(e, "vector", n)
            n = kb.inc("V", e.tensor_tensor(t["hb"][:, tc * D:(tc + 1) * D],
                                            t["dm2"][:], t["bc2"][:], OP.add))
            kb.wait(e, "vector", n)
    ev_h = kb.ev("V")

    @blk.scalar
    def _(e, ev_h=ev_h):
        kb.wait(e, "scalar", ev_h)
        for tc in range(NT):
            src = AP(t["hb"], tc * D, [[NT * D, 128], [1, D]])
            dst = AP(t["hF"], tc * D, [[NT * D, 128], [128, DC], [1, 128]])
            kb.dmat(e, dst, src)
    return kb.ev("D2")


def rhsF(kb, buf, fc):
    return AP(kb.t[buf], fc * 128, [[NT * D, 128], [D, NT], [1, 128]])


def emit_qkv(kb, ev_hF):
    t = kb.t
    blk = kb.block
    pb = [t[f"pb{i}"] for i in range(8)]
    stage_ev = {}
    use_ev = {}

    for j in range(12):
        par = j % 2

        @blk.sync
        def _(e, j=j, par=par):
            if j >= 2:
                kb.wait(e, "sync", use_ev.get(j - 2))
            else:
                kb.wait(e, "sync", kb.ev_store.get("wstf", {}).get(par))
            src = AP(t["WATT"].tensor, j * 512, [[3 * D, 128], [128 * 3 * D, DC], [1, 512]])
            dst = AP(t["wst"], par * DC * 512, [[2 * DC * 512, 128], [512, DC], [1, 512]])
            stage_ev[j] = kb.dma(e, "sync", "D1", dst, src)

        if j < 8:
            dstb = t["qb"] if j < 4 else t["kb"]
            ob = j if j < 4 else j - 4
            for oc4 in range(4):
                gs = j * 4 + oc4
                bank = gs % 2

                @blk.tensor
                def _(e, j=j, par=par, oc4=oc4, bank=bank, gs=gs):
                    kb.wait(e, "tensor", stage_ev[j], ev_hF)
                    kb.wait(e, "tensor", use_ev.get(("ev", gs - 2)))
                    for fc in range(DC):
                        lhs = AP(t["wst"], par * DC * 512 + fc * 512 + oc4 * 128,
                                 [[2 * DC * 512, 128], [1, 128]])
                        kb.inc("T", e.matmul(pb[bank][:], lhs, rhsF(kb, "hF", fc),
                                             start=(fc == 0), stop=(fc == DC - 1)))
                ev_mm = kb.ev("T")

                @blk.scalar
                def _(e, j=j, oc4=oc4, bank=bank, dstb=dstb, ob=ob, gs=gs, ev_mm=ev_mm):
                    kb.wait(e, "scalar", ev_mm)
                    oc = ob * 4 + oc4
                    use_ev[("ev", gs)] = kb.inc(
                        "A", e.copy(dstb[:, oc * 512:(oc + 1) * 512], pb[bank][:]))
                if oc4 == 3:
                    use_ev[j] = kb.ev("A")
        else:
            vo = j - 8
            for tc in range(NT):
                gv = vo * 4 + tc
                bank = 2 + gv % 2

                @blk.tensor
                def _(e, j=j, par=par, tc=tc, bank=bank, gv=gv):
                    kb.wait(e, "tensor", stage_ev[j], ev_hF)
                    kb.wait(e, "tensor", use_ev.get(("vev", gv - 2)))
                    for fc in range(DC):
                        lhs = AP(t["hF"], tc * D + fc * 128, [[NT * D, 128], [1, 128]])
                        rhs = AP(t["wst"], par * DC * 512 + fc * 512,
                                 [[2 * DC * 512, 128], [1, 512]])
                        kb.inc("T", e.matmul(pb[bank][:], lhs, rhs,
                                             start=(fc == 0), stop=(fc == DC - 1)))
                ev_mm = kb.ev("T")

                @blk.scalar
                def _(e, j=j, tc=tc, vo=vo, bank=bank, gv=gv, ev_mm=ev_mm):
                    kb.wait(e, "scalar", ev_mm)
                    use_ev[("vev", gv)] = kb.inc(
                        "A", e.copy(t["vb"][:, tc * D + vo * 512: tc * D + (vo + 1) * 512],
                                    pb[bank][:]))
                if tc == NT - 1:
                    use_ev[j] = kb.ev("A")
    kb.ev_store["wstf"] = {0: use_ev[10], 1: use_ev[11]}
    return kb.allev()


def emit_attention(kb, st, ev_qkv):
    t = kb.t
    blk = kb.block
    pb = [t[f"pb{i}"] for i in range(8)]
    has_carry = (st % NST) != 0
    ms = ([-1] if has_carry else []) + list(range(NT))
    at_ev = {}

    for mi, m in enumerate(ms):
        bank = 6 + mi % 2
        wid = 128 if (m == -1 or m == NT - 1) else 256
        ioff = 0 if m == -1 else m * 128

        @blk.tensor
        def _(e, m=m, mi=mi, bank=bank, wid=wid, ioff=ioff):
            kb.wait(e, "tensor", ev_qkv)
            kb.wait(e, "tensor", at_ev.get(mi - 2))
            for oc in range(DC):
                if m == -1:
                    lhs = AP(t["kc"], oc * 128, [[DC * 128, 128], [1, 128]])
                else:
                    lhs = AP(t["kb"], oc * 512 + m * 128, [[DC * ST, 128], [1, 128]])
                rhs = AP(t["qb"], oc * 512 + ioff, [[DC * ST, 128], [1, wid]])
                kb.inc("T", e.matmul(pb[bank][:, 0:wid], lhs, rhs,
                                     start=(oc == 0), stop=(oc == DC - 1)))
        ev_sim = kb.ev("T")

        @blk.scalar
        def _(e, mi=mi, bank=bank, wid=wid, ev_sim=ev_sim):
            kb.wait(e, "scalar", ev_sim)
            kb.wait(e, "scalar", at_ev.get(mi - 2))
            kb.inc("A", e.activation(t["rb"][:, (mi % 2) * 256:(mi % 2) * 256 + wid],
                                     pb[bank][:, 0:wid], AF.Relu))
        ev_r = kb.ev("A")

        @blk.vector
        def _(e, m=m, mi=mi, bank=bank, wid=wid, ev_r=ev_r):
            kb.wait(e, "vector", ev_r)
            n = kb.inc("V", e.scalar_tensor_tensor(
                t["dm2"][:, 0:wid], pb[bank][:, 0:wid], 1.0 / (128.0 * 128.0),
                t["rb"][:, (mi % 2) * 256:(mi % 2) * 256 + wid], OP.mult, OP.mult))
            kb.wait(e, "vector", n)
            if m == -1:
                n = kb.inc("V", e.tensor_tensor(t["atm"][:, mi * 256:mi * 256 + 128],
                                                t["dm2"][:, 0:128], t["mPs"][:], OP.mult))
                kb.wait(e, "vector", n)
            else:
                n = kb.inc("V", e.tensor_tensor(t["atm"][:, mi * 256:mi * 256 + 128],
                                                t["dm2"][:, 0:128], t["mCs"][:], OP.mult))
                kb.wait(e, "vector", n)
                if wid == 256:
                    n = kb.inc("V", e.tensor_tensor(t["atm"][:, mi * 256 + 128:mi * 256 + 256],
                                                    t["dm2"][:, 128:256], t["mPs"][:], OP.mult))
                    kb.wait(e, "vector", n)
            at_ev[mi] = kb.ev("V")
    ev_at = kb.ev("V")

    ev_avec = {}
    for ec in range(DC):
        bank = (ec % 2)

        @blk.tensor
        def _(e, ec=ec, bank=bank):
            kb.wait(e, "tensor", ev_at)
            kb.wait(e, "tensor", ev_avec.get(ec - 2))
            first = True
            for mi, m in enumerate(ms):
                if m == -1:
                    lhs = AP(t["vc"], ec * 128, [[D, 128], [1, 128]])
                    rhs = AP(t["atm"], mi * 256, [[5 * 256, 128], [1, 128]])
                    kb.inc("T", e.matmul(pb[bank][:, 0:128], lhs, rhs,
                                         start=first, stop=False))
                else:
                    wid = 128 if m == NT - 1 else 256
                    lhs = AP(t["vb"], m * D + ec * 128, [[NT * D, 128], [1, 128]])
                    rhs = AP(t["atm"], mi * 256, [[5 * 256, 128], [1, wid]])
                    kb.inc("T", e.matmul(pb[bank][:, m * 128:m * 128 + wid], lhs, rhs,
                                         start=first, stop=(m == NT - 1)))
                first = False
        ev_mm = kb.ev("T")

        @blk.scalar
        def _(e, ec=ec, bank=bank, ev_mm=ev_mm):
            kb.wait(e, "scalar", ev_mm)
            ev_avec[ec] = kb.inc("A", e.copy(t["yb"][:, ec * 512:(ec + 1) * 512], pb[bank][:]))
    ev_y = kb.ev("A")

    @blk.vector
    def _(e):
        kb.wait(e, "vector", ev_y)
        for oc in range(DC):
            n = kb.inc("V", e.tensor_copy(t["kc"][:, oc * 128:(oc + 1) * 128],
                                          t["kb"][:, oc * 512 + 384: oc * 512 + 512]))
        for qv in range(4):
            n = kb.inc("V", e.tensor_copy(t["vc"][:, qv * 512:(qv + 1) * 512],
                                          t["vb"][:, 3 * D + qv * 512: 3 * D + (qv + 1) * 512]))
        kb.wait(e, "vector", n)
    return ev_y, kb.ev("V")


def emit_oproj(kb, st, ev_y, ev_ga):
    t = kb.t
    blk = kb.block
    pb = [t[f"pb{i}"] for i in range(8)]
    stage_ev = {}
    done_ev = {}
    for qo in range(4):
        par = qo % 2

        @blk.sync
        def _(e, qo=qo, par=par):
            if qo >= 2:
                kb.wait(e, "sync", done_ev.get(qo - 2))
            else:
                kb.wait(e, "sync", kb.ev_store.get("wstf", {}).get(par))
            src = AP(t["WOs"].tensor, qo * 512, [[D, 128], [128 * D, DC], [1, 512]])
            dst = AP(t["wst"], par * DC * 512, [[2 * DC * 512, 128], [512, DC], [1, 512]])
            stage_ev[qo] = kb.dma(e, "sync", "D1", dst, src)

        for i in range(NT):
            bank = 4 + i % 2

            @blk.tensor
            def _(e, qo=qo, par=par, i=i, bank=bank):
                kb.wait(e, "tensor", stage_ev[qo], ev_y)
                kb.wait(e, "tensor", done_ev.get((qo, i - 2)))
                for ec in range(DC):
                    lhs = AP(t["yb"], ec * 512 + i * 128, [[DC * ST, 128], [1, 128]])
                    rhs = AP(t["wst"], par * DC * 512 + ec * 512,
                             [[2 * DC * 512, 128], [1, 512]])
                    kb.inc("T", e.matmul(pb[bank][:], lhs, rhs,
                                         start=(ec == 0), stop=(ec == DC - 1)))
            ev_mm = kb.ev("T")

            @blk.vector
            def _(e, qo=qo, i=i, bank=bank, ev_mm=ev_mm):
                kb.wait(e, "vector", ev_mm, ev_ga)
                n = kb.inc("V", e.tensor_tensor(t["dmp"][:, 0:512], pb[bank][:],
                                                t["bc3"][:, qo * 512:(qo + 1) * 512], OP.mult))
                kb.wait(e, "vector", n)
                xs = t["xb"][:, i * D + qo * 512: i * D + (qo + 1) * 512]
                n = kb.inc("V", e.tensor_tensor(xs, xs, t["dmp"][:, 0:512], OP.add))
                kb.wait(e, "vector", n)
                done_ev[(qo, i)] = n
        done_ev[qo] = kb.ev("V")
    kb.ev_store["wstf"] = {0: done_ev[2], 1: done_ev[3]}
    return kb.ev("V")


ROUND_PERM1 = [[8, 16], [1, 8], [128, 16]]


def emit_quant1(kb, tc, ev_y1):
    t = kb.t
    blk = kb.block
    ysl = t["y1"][:, tc * D:(tc + 1) * D]

    @blk.vector
    def _(e, tc=tc, ysl=ysl):
        kb.wait(e, "vector", ev_y1)
        n = kb.inc("V", e.tensor_reduce(t["r_t0"][:], ysl, AX, OP.max,
                                        apply_absolute_value=True))
        kb.wait(e, "vector", n)
        n = kb.inc("V", e.tensor_scalar_max(t["r_t1"][:], t["r_t0"][:], CL1))
        kb.wait(e, "vector", n)
        n = kb.inc("V", e.reciprocal(t["r_t2"][:], t["r_t1"][:]))
        kb.wait(e, "vector", n)
        n = kb.inc("V", e.tensor_scalar_mul(t["r_sc"][:, tc:tc + 1], t["r_t2"][:], 127.0))
        kb.wait(e, "vector", n)
        n = kb.inc("V", e.tensor_scalar_mul(t["r_t2"][:], t["r_t1"][:], RT2048 / 127.0))
        kb.wait(e, "vector", n)
        n = kb.inc("V", e.tensor_tensor(t["r_dg"][:, tc:tc + 1], t["r_t2"][:],
                                        t["wsc"][:, 0:1], OP.mult))
        kb.wait(e, "vector", n)
        n = kb.inc("V", e.tensor_tensor(t["r_cu"][:, tc:tc + 1], t["r_t2"][:],
                                        t["wsc"][:, 2:3], OP.mult))
        kb.wait(e, "vector", n)
    ev_rows = kb.ev("V")

    @blk.scalar
    def _(e, tc=tc, ysl=ysl, ev_rows=ev_rows):
        kb.wait(e, "scalar", ev_rows)
        n = kb.inc("A", e.activation(t["rib"][:], ysl, AF.Identity,
                                     scale=t["r_sc"][:, tc:tc + 1]))
        kb.wait(e, "scalar", n)
    ev_cast = kb.ev("A")

    @blk.vector
    def _(e, tc=tc, ysl=ysl, ev_cast=ev_cast):
        kb.wait(e, "vector", ev_cast)
        n = kb.inc("V", e.tensor_copy(t["rfb"][:], t["rib"][:]))
        kb.wait(e, "vector", n)
        n = kb.inc("V", e.scalar_tensor_tensor(t["ddb"][:], ysl, t["r_sc"][:, tc:tc + 1],
                                               t["rfb"][:], OP.mult, OP.subtract))
        kb.wait(e, "vector", n)
        n = kb.inc("V", e.scalar_tensor_tensor(t["ttb"][:], t["ddb"][:], 0.5,
                                               t["rfb"][:], OP.is_ge, OP.add))
        kb.wait(e, "vector", n)
        dst = AP(t["lvb"], tc * D, [[NT * D, 128]] + ROUND_PERM1)
        n = kb.inc("V", e.scalar_tensor_tensor(dst, t["ddb"][:], -0.5,
                                               t["ttb"][:], OP.is_le, OP.subtract))
        kb.wait(e, "vector", n)
    return kb.ev("V")


def emit_fwht1(kb, ev_hF):
    t = kb.t
    blk = kb.block
    pb = [t[f"pb{i}"] for i in range(8)]

    ev_z = {}
    for c in range(DC):
        bank = c % 2

        @blk.tensor
        def _(e, c=c, bank=bank):
            kb.wait(e, "tensor", ev_hF)
            kb.wait(e, "tensor", ev_z.get(c - 2))
            kb.inc("T", e.matmul(pb[bank][:], t["H128s"][:], rhsF(kb, "hF", c),
                                 start=True, stop=True))
        ev_mm = kb.ev("T")

        @blk.scalar
        def _(e, c=c, bank=bank, ev_mm=ev_mm):
            kb.wait(e, "scalar", ev_mm)
            ev_z[c] = kb.inc("A", e.copy(t["z1"][:, c * 512:(c + 1) * 512], pb[bank][:]))
    ev_z1 = kb.ev("A")

    @blk.gpsimd
    def _(e, ev_z1=ev_z1):
        kb.wait(e, "gpsimd", ev_z1)
        kb.dma(e, "gpsimd", "G", AP(t["ZB"].tensor, 0, [[I8K, 128], [1, I8K]]), t["z1"][:])
    ev_st = kb.ev("G")

    @blk.gpsimd
    def _(e, ev_st=ev_st):
        kb.wait(e, "gpsimd", ev_st)
        for pl in range(8):
            src = AP(t["ZB"].tensor, pl * I8K, [[512, 16], [8 * I8K, 16], [1, 512]])
            dst = AP(t["zl2"], pl * 16 * I8K, [[I8K, 16], [512, 16], [1, 512]])
            kb.dma(e, "gpsimd", "G", dst, src)
    ev_zl = kb.ev("G")

    ev_q = {}
    for tsub in range(NT):
        for phg in range(4):
            gq = tsub * 4 + phg
            bank = 2 + (gq % 2)

            @blk.tensor
            def _(e, tsub=tsub, phg=phg, bank=bank, gq=gq, ev_zl=ev_zl):
                kb.wait(e, "tensor", ev_zl)
                kb.wait(e, "tensor", ev_q.get(gq - 2))
                for s in range(4):
                    ph = phg * 4 + s
                    lhs = AP(t["zl2"], ph * 512 + tsub * 128, [[I8K, 128], [1, 128]])
                    kb.inc("T", e.matmul(pb[bank][:, s * 128:(s + 1) * 128], lhs,
                                         t["M16s"][:], start=True, stop=True))
            ev_mm = kb.ev("T")

            @blk.scalar
            def _(e, tsub=tsub, phg=phg, bank=bank, gq=gq, ev_mm=ev_mm):
                kb.wait(e, "scalar", ev_mm)
                ev_q[gq] = kb.inc(
                    "A", e.copy(t["y1"][:, tsub * D + phg * 512: tsub * D + (phg + 1) * 512],
                                pb[bank][:]))
    return kb.ev("A")


def emit_lvF(kb, ev_lv):
    t = kb.t
    blk = kb.block

    @blk.scalar
    def _(e, ev_lv=ev_lv):
        kb.wait(e, "scalar", ev_lv)
        for tc in range(NT):
            src = AP(t["lvb"], tc * D, [[NT * D, 128], [1, D]])
            dst = AP(t["lvF"], tc * D, [[NT * D, 128], [128, DC], [1, 128]])
            kb.dmat(e, dst, src)
    return kb.ev("D2")


def emit_gateup(kb, ev_lvF, ev_rows):
    t = kb.t
    blk = kb.block
    pb = [t[f"pb{i}"] for i in range(8)]
    stage_ev = {}
    mid_ev = {}
    silu_ev = {}
    mstore_ev = {}

    step = 0
    for gc in range(16):
        for proj in range(2):
            par = step % 2
            wsrc = t["WGs"] if proj == 0 else t["WUs"]

            @blk.sync
            def _(e, gc=gc, proj=proj, par=par, wsrc=wsrc, step=step):
                if step >= 2:
                    kb.wait(e, "sync", mid_ev.get(step - 2), silu_ev.get(step - 2))
                else:
                    kb.wait(e, "sync", kb.ev_store.get("wstf", {}).get(par))
                src = AP(wsrc.tensor, gc * 512, [[I8K, 128], [128 * I8K, DC], [1, 512]])
                dst = AP(t["wst"], par * DC * 512, [[2 * DC * 512, 128], [512, DC], [1, 512]])
                stage_ev[step] = kb.dma(e, "sync", "D1", dst, src)

            for tc in range(NT):
                bank = proj * 4 + tc

                @blk.tensor
                def _(e, gc=gc, proj=proj, par=par, tc=tc, bank=bank, step=step):
                    kb.wait(e, "tensor", stage_ev[step], ev_lvF)
                    kb.wait(e, "tensor", silu_ev.get((step - 2, tc)) if proj == 0
                            else mid_ev.get((step - 2, tc)))
                    for fc in range(DC):
                        lhs = AP(t["lvF"], tc * D + fc * 128, [[NT * D, 128], [1, 128]])
                        rhs = AP(t["wst"], par * DC * 512 + fc * 512,
                                 [[2 * DC * 512, 128], [1, 512]])
                        kb.inc("T", e.matmul(pb[bank][:], lhs, rhs,
                                             start=(fc == 0), stop=(fc == DC - 1)))
                ev_mm = kb.ev("T")

                if proj == 0:
                    @blk.scalar
                    def _(e, tc=tc, bank=bank, step=step, ev_mm=ev_mm):
                        kb.wait(e, "scalar", ev_mm, ev_rows)
                        kb.wait(e, "scalar", mid_ev.get((step - 1, tc)))
                        silu_ev[(step, tc)] = kb.inc("A", e.activation(
                            t["silu"][:, tc * 512:(tc + 1) * 512], pb[bank][:],
                            AF.Silu, scale=t["r_dg"][:, tc:tc + 1]))
                    if tc == NT - 1:
                        silu_ev[step] = kb.ev("A")
                else:
                    @blk.vector
                    def _(e, gc=gc, tc=tc, bank=bank, step=step, ev_mm=ev_mm):
                        kb.wait(e, "vector", ev_mm, silu_ev[(step - 1, tc)])
                        kb.wait(e, "vector", mstore_ev.get(gc // 4 - 1))
                        mid_ev[(step, tc)] = kb.inc("V", e.tensor_tensor(
                            t["mst"][:, tc * D + (gc % 4) * 512: tc * D + (gc % 4 + 1) * 512],
                            t["silu"][:, tc * 512:(tc + 1) * 512], pb[bank][:], OP.mult))
                    if tc == NT - 1:
                        mid_ev[step] = kb.ev("V")
            step += 1

        if gc % 4 == 3:
            ev_m = kb.ev("V")

            @blk.gpsimd
            def _(e, gc=gc, ev_m=ev_m):
                kb.wait(e, "gpsimd", ev_m)
                for tc in range(NT):
                    src = AP(t["mst"], tc * D, [[NT * D, 128], [1, D]])
                    dst = AP(t["MIDH"].tensor, (tc * 128) * I8K + (gc - 3) * 512,
                             [[I8K, 128], [1, D]])
                    kb.dma(e, "gpsimd", "G", dst, src)
            mstore_ev[gc // 4] = kb.ev("G")
    kb.ev_store["wstf"] = {0: silu_ev[30], 1: mid_ev[31]}
    return kb.ev("G")


ROUND_PERM2 = [[2, 16], [1, 2], [128, 64]]


def emit_fwht2_down(kb, st, ev_mid, ev_gb):
    t = kb.t
    blk = kb.block
    pb = [t[f"pb{i}"] for i in range(8)]
    nc = kb.nc
    lv2h_ev = {}

    for tc in range(NT):
        @blk.scalar
        def _(e, tc=tc):
            kb.wait(e, "scalar", ev_mid)
            kb.wait(e, "scalar", lv2h_ev.get(tc - 1))
            src = AP(t["MIDH"].tensor, tc * 128 * I8K, [[I8K, 128], [1, I8K]])
            dst = AP(t["mf"], 0, [[I8K, 128], [128, 64], [1, 128]])
            kb.dmat(e, dst, src)
        ev_t2 = kb.ev("D2")

        ev_z = {}
        for cg in range(16):
            bank = cg % 2

            @blk.tensor
            def _(e, cg=cg, bank=bank, ev_t2=ev_t2):
                kb.wait(e, "tensor", ev_t2)
                kb.wait(e, "tensor", ev_z.get(cg - 2))
                rhs = AP(t["mf"], cg * 512, [[I8K, 128], [1, 512]])
                kb.inc("T", e.matmul(pb[bank][:], t["H128s"][:], rhs, start=True, stop=True))
            ev_mm = kb.ev("T")

            @blk.scalar
            def _(e, cg=cg, bank=bank, ev_mm=ev_mm):
                kb.wait(e, "scalar", ev_mm)
                ev_z[cg] = kb.inc("A", e.copy(t["z1"][:, cg * 512:(cg + 1) * 512], pb[bank][:]))
        ev_z1 = kb.ev("A")

        @blk.gpsimd
        def _(e, ev_z1=ev_z1):
            kb.wait(e, "gpsimd", ev_z1)
            kb.dma(e, "gpsimd", "G", AP(t["ZB"].tensor, 0, [[I8K, 128], [1, I8K]]), t["z1"][:])
        ev_st = kb.ev("G")

        @blk.gpsimd
        def _(e, ev_st=ev_st):
            kb.wait(e, "gpsimd", ev_st)
            for p0 in range(2):
                src = AP(t["ZB"].tensor, p0 * I8K, [[128, 64], [2 * I8K, 64], [1, 128]])
                dst = AP(t["zl2"], p0 * 64 * I8K, [[I8K, 64], [128, 64], [1, 128]])
                kb.dma(e, "gpsimd", "G", dst, src)
        ev_zl = kb.ev("G")

        ev_q = {}
        for phg in range(16):
            bank = 2 + phg % 2

            @blk.tensor
            def _(e, phg=phg, bank=bank, ev_zl=ev_zl):
                kb.wait(e, "tensor", ev_zl)
                kb.wait(e, "tensor", ev_q.get(phg - 2))
                for s in range(4):
                    ph = phg * 4 + s
                    lhs = AP(t["zl2"], ph * 128, [[I8K, 128], [1, 128]])
                    kb.inc("T", e.matmul(pb[bank][:, s * 128:(s + 1) * 128], lhs,
                                         t["M64s"][:], start=True, stop=True))
            ev_mm = kb.ev("T")

            @blk.scalar
            def _(e, phg=phg, bank=bank, ev_mm=ev_mm):
                kb.wait(e, "scalar", ev_mm)
                ev_q[phg] = kb.inc("A", e.copy(t["y2"][:, phg * 512:(phg + 1) * 512],
                                               pb[bank][:]))
        ev_y2 = kb.ev("A")

        @blk.vector
        def _(e, tc=tc, ev_y2=ev_y2):
            kb.wait(e, "vector", ev_y2)
            n = kb.inc("V", e.tensor_reduce(t["r_t0"][:], t["y2"][:], AX, OP.max,
                                            apply_absolute_value=True))
            kb.wait(e, "vector", n)
            n = kb.inc("V", e.tensor_tensor(t["r_t1"][:], t["r_t0"][:],
                                            t["r_cu"][:, tc:tc + 1], OP.mult))
            kb.wait(e, "vector", n)
            n = kb.inc("V", e.tensor_scalar_mul(t["r_t1"][:], t["r_t1"][:], RT8192))
            kb.wait(e, "vector", n)
            n = kb.inc("V", e.tensor_scalar_max(t["r_t2"][:], t["r_t1"][:], CL2))
            kb.wait(e, "vector", n)
            n = kb.inc("V", e.reciprocal(t["r_t0"][:], t["r_t2"][:]))
            kb.wait(e, "vector", n)
            n = kb.inc("V", e.tensor_tensor(t["r_t1"][:], t["r_t0"][:],
                                            t["r_cu"][:, tc:tc + 1], OP.mult))
            kb.wait(e, "vector", n)
            n = kb.inc("V", e.tensor_scalar_mul(t["r_s"][:], t["r_t1"][:], 127.0 * RT8192))
            kb.wait(e, "vector", n)
            n = kb.inc("V", e.tensor_tensor(t["r_t1"][:], t["r_t2"][:],
                                            t["wsc"][:, 4:5], OP.mult))
            kb.wait(e, "vector", n)
            n = kb.inc("V", e.tensor_scalar_mul(t["r_d2"][:], t["r_t1"][:], 1.0 / 127.0))
            kb.wait(e, "vector", n)
        ev_rows = kb.ev("V")

        prev_cp = {}
        for sl in range(4):
            ysl = t["y2"][:, sl * D:(sl + 1) * D]

            @blk.scalar
            def _(e, sl=sl, ysl=ysl, ev_rows=ev_rows):
                kb.wait(e, "scalar", ev_rows)
                kb.wait(e, "scalar", prev_cp.get(sl - 1))
                n = kb.inc("A", e.activation(t["rib"][:], ysl, AF.Identity,
                                             scale=t["r_s"][:, 0:1]))
                kb.wait(e, "scalar", n)
            ev_cast = kb.ev("A")

            @blk.vector
            def _(e, sl=sl, ysl=ysl, ev_cast=ev_cast):
                kb.wait(e, "vector", ev_cast)
                n = kb.inc("V", e.tensor_copy(t["rfb"][:], t["rib"][:]))
                prev_cp[sl] = n
                kb.wait(e, "vector", n)
                n = kb.inc("V", e.scalar_tensor_tensor(t["ddb"][:], ysl, t["r_s"][:, 0:1],
                                                       t["rfb"][:], OP.mult, OP.subtract))
                kb.wait(e, "vector", n)
                n = kb.inc("V", e.scalar_tensor_tensor(t["ttb"][:], t["ddb"][:], 0.5,
                                                       t["rfb"][:], OP.is_ge, OP.add))
                kb.wait(e, "vector", n)
                dst = AP(t["lv2"], sl * 32, [[I8K, 128]] + ROUND_PERM2)
                n = kb.inc("V", e.scalar_tensor_tensor(dst, t["ddb"][:], -0.5,
                                                       t["ttb"][:], OP.is_le, OP.subtract))
                kb.wait(e, "vector", n)
        ev_lvr = kb.ev("V")

        if DEBUG_TAPS and st == DBG_ST and tc == 0:
            @blk.sync
            def _(e, ev_lvr=ev_lvr):
                kb.wait(e, "sync", ev_lvr)
                kb.dma(e, "sync", "D1", AP(t["ty2"].tensor, 0, [[I8K, 128], [1, I8K]]),
                       t["y2"][:])
                kb.dma(e, "sync", "D1", AP(t["tlv2"].tensor, 0, [[I8K, 128], [1, I8K]]),
                       t["lv2"][:])
            ev_lvr = kb.ev("D1")

        @blk.vector
        def _(e, ev_lvr=ev_lvr):
            kb.wait(e, "vector", ev_lvr)
            n = kb.inc("V", e.tensor_scalar(t["lv2"][:], t["lv2"][:], t["r_d2"][:, 0:1],
                                            None, OP.mult))
            kb.wait(e, "vector", n)
        ev_lv2 = kb.ev("V")

        @blk.scalar
        def _(e, ev_lv2=ev_lv2):
            kb.wait(e, "scalar", ev_lv2)
            src = AP(t["lv2"], 0, [[I8K, 128], [1, I8K]])
            dst = AP(t["lv2F"], 0, [[I8K, 128], [128, 64], [1, 128]])
            kb.dmat(e, dst, src)
        ev_t3 = kb.ev("D2")

        @blk.gpsimd
        def _(e, tc=tc, ev_t3=ev_t3):
            kb.wait(e, "gpsimd", ev_t3)
            src = AP(t["lv2F"], 0, [[I8K, 128], [128, 64], [1, 128]])
            dst = AP(t["LV2H"].tensor, tc * 128, [[ST, 128], [128 * ST, 64], [1, 128]])
            lv2h_ev[tc] = kb.dma(e, "gpsimd", "G", dst, src)
    ev_lv2h = kb.ev("G")

    stg_ev = {}
    done_ev = {}
    for oh in range(2):
        for gc in range(64):
            par = gc % 2

            @blk.sync
            def _(e, oh=oh, gc=gc, par=par, ev_lv2h=ev_lv2h):
                kb.wait(e, "sync", ev_lv2h)
                if gc >= 2:
                    kb.wait(e, "sync", done_ev.get((oh, gc - 2)))
                src1 = AP(t["LV2H"].tensor, gc * 128 * ST, [[ST, 128], [1, ST]])
                dst1 = AP(t["mf"], par * 2048, [[I8K, 128], [1, ST]])
                kb.dma(e, "sync", "D1", dst1, src1)
                src2 = AP(t["WDs"].tensor, gc * 128 * D + oh * 1024, [[D, 128], [1, 1024]])
                dst2 = AP(t["mf"], par * 2048 + 512, [[I8K, 128], [1, 1024]])
                kb.dma(e, "sync", "D1", dst2, src2)
            ev_stg = kb.ev("D1")

            @blk.tensor
            def _(e, oh=oh, gc=gc, par=par, ev_stg=ev_stg):
                kb.wait(e, "tensor", ev_stg)
                if gc == 0:
                    kb.wait(e, "tensor", done_ev.get((oh - 1, "all")))
                for tc in range(NT):
                    lhs = AP(t["mf"], par * 2048 + tc * 128, [[I8K, 128], [1, 128]])
                    for q in range(2):
                        rhs = AP(t["mf"], par * 2048 + 512 + q * 512, [[I8K, 128], [1, 512]])
                        kb.inc("T", e.matmul(pb[tc * 2 + q][:], lhs, rhs,
                                             start=(gc == 0), stop=(gc == 63)))
                done_ev[(oh, gc)] = kb.ev("T")
        ev_dn = kb.ev("T")

        @blk.vector
        def _(e, oh=oh, ev_dn=ev_dn):
            kb.wait(e, "vector", ev_dn, ev_gb)
            for tc in range(NT):
                for q in range(2):
                    col = oh * 1024 + q * 512
                    n = kb.inc("V", e.tensor_tensor(t["dmp"][:, 0:512], pb[tc * 2 + q][:],
                                                    t["bc3"][:, col:col + 512], OP.mult))
                    kb.wait(e, "vector", n)
                    xs = t["xb"][:, tc * D + col: tc * D + col + 512]
                    n = kb.inc("V", e.tensor_tensor(xs, xs, t["dmp"][:, 0:512], OP.add))
                    kb.wait(e, "vector", n)
        done_ev[(oh, "all")] = kb.ev("V")
    return kb.ev("V")


def emit_supertile(kb, st):
    t = kb.t
    blk = kb.block
    b = st // NST
    tst = (st % NST) * ST
    pre = kb.allev()

    @blk.sync
    def _(e):
        kb.wait(e, "sync", *pre)
        src = AP(t["xin"].tensor, b * T * D + tst * D, [[D, 128], [128 * D, NT], [1, D]])
        dst = AP(t["xb"], 0, [[NT * D, 128], [D, NT], [1, D]])
        kb.dma(e, "sync", "D1", dst, src)
    ev_x = kb.ev("D1")

    ev_bc = emit_bcast(kb, b, "ln1", waits=(pre,))

    ev_hF = emit_ln_mod(kb, (ev_x, ev_bc))
    if DEBUG_TAPS and st == DBG_ST:
        @blk.sync
        def _(e):
            kb.wait(e, "sync", kb.ev("V"), kb.ev("D2"))
            kb.dma(e, "sync", "D1", AP(t["thb"].tensor, 0, [[I8K, 128], [1, I8K]]), t["hb"][:])
    ev_qkv = emit_qkv(kb, ev_hF)
    if DEBUG_TAPS and st == DBG_ST:
        @blk.sync
        def _(e):
            kb.wait(e, "sync", *ev_qkv)
            for n, b_ in (("tqb", "qb"), ("tkb", "kb"), ("tvb", "vb")):
                kb.dma(e, "sync", "D1", AP(t[n].tensor, 0, [[I8K, 128], [1, I8K]]), t[b_][:])
            kb.dma(e, "sync", "D1", AP(t["tkc"].tensor, 0, [[I8K, 128], [1, DC * 128]]), t["kc"][:])
            kb.dma(e, "sync", "D1", AP(t["tvc"].tensor, 0, [[I8K, 128], [1, D]]), t["vc"][:])
    ev_y, ev_carry = emit_attention(kb, st, ev_qkv)
    if DEBUG_TAPS and st == DBG_ST:
        @blk.sync
        def _(e):
            kb.wait(e, "sync", ev_y, kb.ev("V"))
            kb.dma(e, "sync", "D1", AP(t["tyb"].tensor, 0, [[I8K, 128], [1, I8K]]), t["yb"][:])
            kb.dma(e, "sync", "D1", AP(t["tat"].tensor, 0, [[I8K, 128], [1, 1280]]),
                   t["atm"][:])
    if DEBUG_TAPS and st == DBG_ST - 1:
        @blk.gpsimd
        def _(e):
            kb.wait(e, "gpsimd", ev_y)
            kb.dma(e, "gpsimd", "G", AP(t["tat"].tensor, 2048, [[I8K, 128], [1, D]]),
                   t["vb"][:, 3 * D:4 * D])

        @blk.gpsimd
        def _(e):
            kb.wait(e, "gpsimd", ev_carry)
            kb.dma(e, "gpsimd", "G", AP(t["tat"].tensor, 4096, [[I8K, 128], [1, D]]),
                   t["vc"][:])
            kb.dma(e, "gpsimd", "G", AP(t["tat"].tensor, 6144, [[I8K, 128], [1, D]]),
                   t["vb"][:, 3 * D:4 * D])
    ev_ga = emit_bcast(kb, b, "ga", waits=(ev_hF,))
    ev_attn = emit_oproj(kb, st, ev_y, ev_ga)

    if ATTN_DONE_ONLY:
        @blk.sync
        def _(e):
            kb.wait(e, "sync", ev_attn)
            src = AP(t["xb"], 0, [[NT * D, 128], [D, NT], [1, D]])
            dst = AP(t["xout"].tensor, b * T * D + tst * D, [[D, 128], [128 * D, NT], [1, D]])
            kb.dma(e, "sync", "D1", dst, src)
        return

    ev_bc2 = emit_bcast(kb, b, "ln2", waits=(ev_attn, kb.ev("V")))
    ev_h2F = emit_ln_mod(kb, (ev_attn, ev_bc2))
    ev_y1 = emit_fwht1(kb, ev_h2F)
    for tc in range(NT):
        emit_quant1(kb, tc, ev_y1)
    if DEBUG_TAPS and st == DBG_ST:
        @blk.sync
        def _(e):
            kb.wait(e, "sync", kb.ev("V"), kb.ev("A"))
            kb.dma(e, "sync", "D1", AP(t["thb"].tensor, 0, [[I8K, 128], [1, I8K]]), t["hb"][:])
            kb.dma(e, "sync", "D1", AP(t["tqb"].tensor, 0, [[I8K, 128], [1, I8K]]), t["y1"][:])
            kb.dma(e, "sync", "D1", AP(t["tkb"].tensor, 0, [[I8K, 128], [1, I8K]]), t["lvb"][:])
    ev_lvF = emit_lvF(kb, kb.ev("V"))
    ev_rows = kb.ev("V")
    ev_mid = emit_gateup(kb, ev_lvF, ev_rows)
    if DEBUG_TAPS and st == DBG_ST:
        @blk.sync
        def _(e):
            kb.wait(e, "sync", ev_mid)
            kb.dma(e, "sync", "D1", AP(t["tmid"].tensor, 0, [[1, 1], [1, ST * I8K]]),
                   AP(t["MIDH"].tensor, 0, [[1, 1], [1, ST * I8K]]))
    ev_gb = emit_bcast(kb, b, "gb", waits=(kb.ev("V"),))
    ev_ffn = emit_fwht2_down(kb, st, ev_mid, ev_gb)

    @blk.sync
    def _(e):
        kb.wait(e, "sync", ev_ffn)
        src = AP(t["xb"], 0, [[NT * D, 128], [D, NT], [1, D]])
        dst = AP(t["xout"].tensor, b * T * D + tst * D, [[D, 128], [128 * D, NT], [1, D]])
        kb.dma(e, "sync", "D1", dst, src)


def kernel(L0_attn_x, L0_attn_w_attn, L0_attn_w_o, L0_ffn_w_gate, L0_ffn_w_up,
           L0_ffn_w_down, L0_c, L0_wmod):
    co = _consts()
    nc = build_program()
    in_maps = []
    for c in range(NCORES):
        m = dict(
            xin=np.ascontiguousarray(L0_attn_x[2 * c:2 * c + 2]),
            wattn=np.ascontiguousarray(L0_attn_w_attn),
            wo=np.ascontiguousarray(L0_attn_w_o),
            wg=np.ascontiguousarray(L0_ffn_w_gate),
            wu=np.ascontiguousarray(L0_ffn_w_up),
            wd=np.ascontiguousarray(L0_ffn_w_down),
            cv=np.ascontiguousarray(L0_c[2 * c:2 * c + 2]),
            wmod=np.ascontiguousarray(L0_wmod),
            **co,
        )
        in_maps.append(m)
    res = run_bass_kernel_spmd(nc, in_maps, core_ids=list(range(NCORES)))
    out = np.empty((B, T, D), dtype=np.float32)
    for c in range(NCORES):
        out[2 * c:2 * c + 2] = res.results[c]["xout"]
    return out
